# version 05 (final)
# baseline (speedup 1.0000x reference)
import numpy as np
from contextlib import ExitStack

import concourse.bass as bass
import concourse.mybir as mybir
from concourse.bass_utils import run_bass_kernel_spmd

F32 = mybir.dt.float32
BF16 = mybir.dt.bfloat16
AF = mybir.ActivationFunctionType
ALU = mybir.AluOpType
AX = mybir.AxisListType

B, T, D = 4, 4096, 1024
H, DH = 16, 64
NPAIR = 8
R_SPAN = 256.0
CSHIFT = 24.0
EPS = 1e-5
DFF, RANK = 4096, 256
NT = T // 128
OWN_T = T // 2
NCH = T // 512
EDEPTH = 2
NCHQ = OWN_T // 512
_DBG = False


def build_program(dbg=False):
    nc = bass.Bass("TRN2", target_bir_lowering=False)

    x_in = nc.dram_tensor("x", [T, D], F32, kind="ExternalInput").ap()
    xres_in = nc.dram_tensor("x_res", [OWN_T, D], F32, kind="ExternalInput").ap()
    wq_in = nc.dram_tensor("wq", [NPAIR, 128, 1024], F32, kind="ExternalInput").ap()
    wk_in = nc.dram_tensor("wk", [NPAIR, 128, 1024], F32, kind="ExternalInput").ap()
    wv_in = nc.dram_tensor("wv", [NPAIR, 128, 1024], F32, kind="ExternalInput").ap()
    wo_in = nc.dram_tensor("wo", [128, 8192], F32, kind="ExternalInput").ap()
    bo_in = nc.dram_tensor("bo", [128, 8], F32, kind="ExternalInput").ap()
    wspan_in = nc.dram_tensor("wspan", [128, 128], F32, kind="ExternalInput").ap()
    bspan_in = nc.dram_tensor("bspan", [1, 16], F32, kind="ExternalInput").ap()
    hh_in = nc.dram_tensor("hhc", [128, 1], F32, kind="ExternalInput").ap()
    u1_in = nc.dram_tensor("u1", [128, 2048], F32, kind="ExternalInput").ap()
    v1_in = nc.dram_tensor("v1", [128, 8192], F32, kind="ExternalInput").ap()
    b1_in = nc.dram_tensor("b1", [128, 32], F32, kind="ExternalInput").ap()
    u2_in = nc.dram_tensor("u2", [128, 8192], F32, kind="ExternalInput").ap()
    v2_in = nc.dram_tensor("v2", [128, 2048], F32, kind="ExternalInput").ap()
    b2_in = nc.dram_tensor("b2", [128, 8], F32, kind="ExternalInput").ap()
    ident_in = nc.dram_tensor("ident", [128, 128], F32, kind="ExternalInput").ap()
    y_out = nc.dram_tensor("y", [OWN_T, D], F32, kind="ExternalOutput").ap()
    dbg_t = {}
    if dbg:
        for nm, shp, dt in [("dbg_xn", [128, 512], BF16), ("dbg_xbar", [128, 8], F32),
                            ("dbg_mgb", [128, 16], F32), ("dbg_kp", [128, 256], BF16),
                            ("dbg_qp", [128, 256], BF16), ("dbg_vp", [128, 256], BF16),
                            ("dbg_mask", [128, 512], BF16), ("dbg_ot", [128, 16384], BF16),
                            ("dbg_stat", [128, 512], F32), ("dbg_den", [1, 1024], F32),
                            ("dbg_rsb", [128, 1024], F32), ("dbg_xn2", [128, 512], BF16),
                            ("dbg_h", [128, 512], BF16),
                            ("dbg_er", [128, 2048], BF16),
                            ("dbg_pr", [128, 2048], BF16),
                            ("dbg_p4", [128, 1024], F32),
                            ("dbg_wo", [128, 1024], BF16),
                            ("dbg_ws2", [128, 1024], F32),
                            ("dbg_o4", [128, 512], F32),
                            ("dbg_boh", [128, 8], F32),
                            ("dbg_ln2", [128, 8], F32),
                            ("dbg_rsb2", [128, 2048], F32),
                            ("dbg_xn2b", [128, 512], BF16),
                            ("dbg_fout", [128, 4096], F32)]:
            dbg_t[nm] = nc.dram_tensor(nm, shp, dt, kind="ExternalOutput").ap()

    stP = ExitStack()
    stA = ExitStack()
    stA1 = ExitStack()

    def mk(stack, name, shape, dt):
        return stack.enter_context(nc.sbuf_tensor(name, shape, dt))

    consts = mk(stP, "consts", [128, 4], F32)
    ones_bf = mk(stP, "ones_bf", [128, 1], BF16)
    onesf = mk(stP, "onesf", [1, 128], F32)
    zeros_bf = mk(stP, "zeros_bf", [128, 128], BF16)
    zclr = mk(stP, "zclr", [128, 512], BF16)
    identf = mk(stP, "identf", [128, 128], F32)
    ident_bf = mk(stP, "ident_bf", [128, 128], BF16)
    xtok = mk(stP, "xtok", [128, 2 * D], BF16)
    sqscr = mk(stP, "sqscr", [128, D], F32)
    srow = mk(stP, "srow", [128, 4], F32)
    mrow = mk(stP, "mrow", [128, 2], F32)
    msq = mk(stP, "msq", [128, 2], F32)
    varr = mk(stP, "varr", [128, 2], F32)
    lnr = mk(stP, "lnr", [128, 2], F32)
    rstd = mk(stP, "rstd", [128, 2], F32)
    xbar = mk(stP, "xbar", [128, 8], F32)
    xbar2 = mk(stP, "xbar2", [128, 8], F32)
    wspan_sb = mk(stP, "wspan_sb", [128, 128], F32)
    bspan_sb = mk(stP, "bspan_sb", [1, 16], F32)
    hh_sb = mk(stP, "hh_sb", [128, 1], F32)
    hh9 = mk(stP, "hh9", [128, 1], F32)
    bo_sb = mk(stP, "bo_sb", [128, 8], F32)
    b1_sb = mk(stP, "b1_sb", [128, 32], F32)
    b2_sb = mk(stP, "b2_sb", [128, 8], F32)
    zl_sb = mk(stP, "zl_sb", [1, 16], F32)
    zexp = mk(stP, "zexp", [1, 16], F32)
    zep1 = mk(stP, "zep1", [1, 16], F32)
    zsig = mk(stP, "zsig", [1, 16], F32)
    mgrow = mk(stP, "mgrow", [1, 16], F32)
    mgb = mk(stP, "mgb", [128, 16], F32)
    ot = mk(stP, "ot", [128, NPAIR * OWN_T], BF16)

    rampT = mk(stA, "rampT", [128, 4096], F32)
    xn1 = mk(stA, "xn1", [128, 8 * T], BF16)
    maskT = mk(stA, "maskT", [128, 2 * 4096], BF16)
    wq_bf = mk(stA, "wq_bf", [128, 1024], BF16)
    wk_bf = mk(stA, "wk_bf", [128, 1024], BF16)
    wv_bf = mk(stA, "wv_bf", [128, 1024], BF16)
    wstage = mk(stA, "wstage", [128, 1024], F32)
    kp = mk(stA, "kp", [128, T], BF16)
    qp = mk(stA, "qp", [128, OWN_T], BF16)
    vp = mk(stA, "vp", [128, T], BF16)
    e_ring = mk(stA, "e_ring", [128, EDEPTH * 1024], BF16)
    p_ring = mk(stA, "p_ring", [128, EDEPTH * 1024], BF16)
    m0r = mk(stA, "m0r", [1, 512], F32)
    z1r = mk(stA, "z1r", [1, 512], F32)
    m1r = mk(stA, "m1r", [1, 512], F32)
    dr0 = mk(stA, "dr0", [1, 512], F32)
    dr1 = mk(stA, "dr1", [1, 512], F32)
    dbc = mk(stA, "dbc", [128, 512], F32)
    stat_sb = mk(stA, "stat_sb", [128, 512], F32)
    xstage = mk(stA1, "xstage", [128, 2 * D], F32)

    pbs = ExitStack()
    pb = [pbs.enter_context(nc.psum_tensor(f"pb{i}", [128, 512], F32))
          for i in range(6)]
    bfA = pbs.enter_context(nc.psum_tensor("bfA", [128, 1024], BF16))
    bfB = pbs.enter_context(nc.psum_tensor("bfB", [128, 1024], BF16))

    sem_names = ["sPE", "sACT", "sDVE", "sPOOL", "sDS", "sDA", "sCC"]
    sst = ExitStack()
    sems = {n: sst.enter_context(nc.semaphore(n)) for n in sem_names}
    cnt = {n: 0 for n in sem_names}

    nc.const_aps.register(F32, -CSHIFT, consts[:, 0:1])
    nc.const_aps.register(F32, EPS, consts[:, 1:2])

    block_cm = nc.Block()
    block = block_cm.__enter__()

    ops = {n: [] for n in ["sync", "scalar", "vector", "tensor", "gpsimd"]}

    def barrier_waits():
        return [(s, cnt[s]) for s in ("sPE", "sACT", "sDVE", "sPOOL")]

    def emit(eng, waits, makeop, inc=None, by=1):
        waits = [w for w in waits if w is not None and w[1] > 0]
        val = None
        if inc is not None:
            cnt[inc] += by
            val = cnt[inc]

        def fn(e, waits=tuple(waits), makeop=makeop, inc=inc, by=by):
            for (s, v) in waits:
                e.wait_ge(sems[s], v)
            instr = makeop(e)
            if inc is not None:
                if inc in ("sACT", "sDVE", "sPOOL", "sPE"):
                    e.maybe_drain_then_inc((sems[inc], by))
                else:
                    instr.then_inc(sems[inc], by)
        ops[eng].append(fn)
        return val

    emit("gpsimd", [], lambda g: g.memset(consts[:, 0:1], -CSHIFT))
    emit("gpsimd", [], lambda g: g.memset(consts[:, 1:2], EPS))
    emit("gpsimd", [], lambda g: g.memset(ones_bf[:, :], 1.0))
    emit("gpsimd", [], lambda g: g.memset(onesf[:, :], 1.0))
    emit("gpsimd", [], lambda g: g.memset(zeros_bf[:, :], 0))
    emit("gpsimd", [], lambda g: g.memset(zclr[:, :], 0))
    for _tz in (kp, qp, vp, e_ring, p_ring, wq_bf, wk_bf, wv_bf, wstage,
                dbc, stat_sb, m0r, z1r, m1r, dr0, dr1):
        emit("gpsimd", [], lambda g, _tz=_tz: g.memset(_tz[:, :], 0))
    for _c0 in range(0, 2 * 4096, 2048):
        emit("gpsimd", [],
             lambda g, _c0=_c0: g.memset(maskT[:, _c0:_c0 + 2048], 0))
    for _c0 in range(0, NPAIR * OWN_T, 4096):
        emit("gpsimd", [],
             lambda g, _c0=_c0: g.memset(ot[:, _c0:_c0 + 4096], 0))
    g_init = emit("gpsimd", [],
                  lambda g: g.iota(rampT[:, :], [[-1, 4096]], base=3968,
                                   channel_multiplier=1,
                                   allow_small_or_imprecise_dtypes=True),
                  inc="sPOOL")

    def dma_a(waits, out, in_):
        return emit("scalar", waits,
                    lambda e, out=out, in_=in_: e.dma_start(out=out, in_=in_),
                    inc="sDA", by=16)

    dma_a([], identf[:, :], ident_in)
    a_hh = dma_a([], hh_sb[:, :], hh_in)
    v_ebias = emit("vector", [("sDA", a_hh)],
                   lambda v: v.tensor_scalar(hh9[:, :], hh_sb[:, :],
                                             -1e9, -CSHIFT,
                                             op0=ALU.mult, op1=ALU.add),
                   inc="sDVE")
    dma_a([], wspan_sb[:, :], wspan_in)
    dma_a([], bspan_sb[:, :], bspan_in)
    dma_a([], bo_sb[:, :], bo_in)
    dma_a([], b1_sb[:, :], b1_in)
    a_small = dma_a([], b2_sb[:, :], b2_in)
    v_identbf = emit("vector", [("sDA", 16)],
                     lambda v: v.tensor_copy(ident_bf[:, :], identf[:, :]),
                     inc="sDVE")

    vXt = {-2: 0, -1: 0}
    pT = {-2: 0, -1: 0}
    vXn = {-2: 0, -1: 0}

    lnchain = {"vV": {-1: 0}}

    def ln_tile(src_ap_fn, tt, dst_copy_fn, extra_first_waits):
        par = tt % 2
        aSq = emit("scalar",
                   extra_first_waits + [("sDVE", vXt[tt - 2]),
                                        ("sDVE", lnchain["vV"][tt - 1])],
                   lambda sc: sc.activation(sqscr[:, :], src_ap_fn(), AF.Square),
                   inc="sACT")
        r1 = emit("vector", extra_first_waits,
                  lambda v: v.reduce_sum(srow[:, 2 * par:2 * par + 1], src_ap_fn(),
                                         axis=AX.X),
                  inc="sDVE")
        r2 = emit("vector", [("sACT", aSq)],
                  lambda v: v.reduce_sum(srow[:, 2 * par + 1:2 * par + 2],
                                         sqscr[:, :], axis=AX.X),
                  inc="sDVE")
        r3 = emit("vector", [("sDVE", r1)],
                  lambda v: v.tensor_scalar_mul(mrow[:, par:par + 1],
                                                srow[:, 2 * par:2 * par + 1],
                                                1.0 / D),
                  inc="sDVE")
        r4 = emit("vector", [("sDVE", r3)],
                  lambda v: v.tensor_tensor(msq[:, par:par + 1],
                                            mrow[:, par:par + 1],
                                            mrow[:, par:par + 1], op=ALU.mult),
                  inc="sDVE")
        vV = emit("vector", [("sDVE", r4), ("sDVE", r2)],
                  lambda v: v.scalar_tensor_tensor(
                      varr[:, par:par + 1], srow[:, 2 * par + 1:2 * par + 2],
                      1.0 / D, msq[:, par:par + 1],
                      op0=ALU.mult, op1=ALU.subtract),
                  inc="sDVE")
        lnchain["vV"][tt] = vV
        aLn = emit("scalar", [("sDVE", vV)],
                   lambda sc: sc.activation(lnr[:, par:par + 1],
                                            varr[:, par:par + 1],
                                            AF.Ln, bias=EPS, scale=1.0),
                   inc="sACT")
        aRs = emit("scalar", [("sACT", aLn)],
                   lambda sc: sc.activation(rstd[:, par:par + 1],
                                            lnr[:, par:par + 1], AF.Exp,
                                            bias=0.0, scale=-0.5),
                   inc="sACT")
        vXt[tt] = emit("vector", [("sACT", aRs), ("sPE", pT[tt - 2])],
                       lambda v: v.tensor_scalar(
                           xtok[:, par * D:par * D + D], src_ap_fn(),
                           mrow[:, par:par + 1], rstd[:, par:par + 1],
                           op0=ALU.subtract, op1=ALU.mult),
                       inc="sDVE")
        bank = bfA if par == 0 else bfB
        for ccx in range(8):
            wl = [("sDVE", vXt[tt]), ("sPOOL", g_init), ("sDVE", vXn[tt - 2]),
                  ("sDVE", v_identbf)] if ccx == 0 else []
            emit("tensor", wl,
                 lambda t, ccx=ccx: t.transpose(
                     bank[:, 128 * ccx:128 * ccx + 128],
                     xtok[:, par * D + 128 * ccx:par * D + 128 * ccx + 128],
                     ident_bf[:, :]),
                 inc="sPE" if ccx == 7 else None)
        pT[tt] = cnt["sPE"]
        for half in range(2):
            vXn[tt] = emit("vector", [("sPE", pT[tt])] if half == 0 else [],
                           lambda v, half=half: dst_copy_fn(v, half, bank),
                           inc="sDVE" if half == 1 else None)
        vXn[tt] = cnt["sDVE"]

    for tt in range(NT):
        slot = tt % 2
        wl = [("sDVE", vXt[tt - 2])] if tt >= 2 else []
        if tt == 0:
            wl = wl + [("sPOOL", g_init)]
        dX = emit("sync", wl,
                  lambda s, tt=tt, slot=slot: s.dma_start(
                      out=xstage[:, slot * D:slot * D + D],
                      in_=x_in[128 * tt:128 * tt + 128, :]),
                  inc="sDS", by=16)

        def src_fn(slot=slot):
            return xstage[:, slot * D:slot * D + D]

        def dst_copy(v, half, bank, tt=tt):
            return v.tensor_copy(
                bass.AP(xn1, T * (4 * half) + 128 * tt,
                        [[8 * T, 128], [T, 4], [1, 128]]),
                bass.AP(bank, 512 * half, [[1024, 128], [128, 4], [1, 128]]))

        ln_tile(src_fn, tt, dst_copy,
                [("sDS", dX)] + ([("sPOOL", g_init)] if tt == 0 else []))
    vXn_all = vXn[NT - 1]

    vrd = 0
    for ccx in range(8):
        vrd = emit("vector", [],
                   lambda v, ccx=ccx: v.reduce_sum(xbar[:, ccx:ccx + 1],
                                                   xn1[:, T * ccx:T * ccx + T],
                                                   axis=AX.X),
                   inc="sDVE")
    vXb = emit("vector", [("sDVE", vrd)],
               lambda v: v.tensor_scalar_mul(xbar2[:, :], xbar[:, :], 1.0 / T),
               inc="sDVE")

    for ccx in range(8):
        emit("tensor",
             [("sDVE", vXb), ("sDA", a_small)] if ccx == 0 else [],
             lambda t, ccx=ccx: t.matmul(
                 pb[0][0:1, 0:16], xbar2[:, ccx:ccx + 1],
                 wspan_sb[:, 16 * ccx:16 * ccx + 16],
                 start=(ccx == 0), stop=(ccx == 7)),
             inc="sPE" if ccx == 7 else None)
    pZ = cnt["sPE"]
    vZ = emit("vector", [("sPE", pZ)],
              lambda v: v.tensor_tensor(zl_sb[0:1, :], pb[0][0:1, 0:16],
                                        bspan_sb[0:1, :], op=ALU.add),
              inc="sDVE")
    aZ = emit("scalar", [("sDVE", vZ)],
              lambda sc: sc.activation(zexp[0:1, :], zl_sb[0:1, :], AF.Exp,
                                       bias=0.0, scale=-1.0),
              inc="sACT")
    z1 = emit("vector", [("sACT", aZ)],
              lambda v: v.tensor_scalar(zep1[0:1, :], zexp[0:1, :], 1.0, None,
                                        op0=ALU.add),
              inc="sDVE")
    z2 = emit("vector", [("sDVE", z1)],
              lambda v: v.reciprocal(zsig[0:1, :], zep1[0:1, :]),
              inc="sDVE")
    vSg = emit("vector", [("sDVE", z2)],
               lambda v: v.tensor_scalar(mgrow[0:1, :], zsig[0:1, :],
                                         float(T / R_SPAN), 1.0,
                                         op0=ALU.mult, op1=ALU.add),
               inc="sDVE")
    pZb = emit("tensor", [("sDVE", vSg)],
               lambda t: t.matmul(pb[0][:, 0:16], onesf[0:1, :], mgrow[0:1, :],
                                  start=True, stop=True),
               inc="sPE")
    vMgb = emit("vector", [("sPE", pZb)],
                lambda v: v.tensor_copy(mgb[:, :], pb[0][:, 0:16]),
                inc="sDVE")

    if dbg:
        emit("sync", [("sDVE", vXn_all)],
             lambda s: s.dma_start(out=dbg_t["dbg_xn"], in_=xn1[:, 0:512]),
             inc="sDS", by=16)
        emit("sync", [("sDVE", vXb)],
             lambda s: s.dma_start(out=dbg_t["dbg_xbar"], in_=xbar[:, :]),
             inc="sDS", by=16)
        emit("sync", [("sDVE", vMgb)],
             lambda s: s.dma_start(out=dbg_t["dbg_mgb"], in_=mgb[:, :]),
             inc="sDS", by=16)

    stA1.close()
    mscr = mk(stA, "mscr", [128, 2048], BF16)
    vf = mk(stA, "vf", [128, T], BF16)

    sS = {}
    sE = {}
    sP = {}
    sQd = {i: 0 for i in range(-EDEPTH, 0)}
    u_global = 0
    vOt_last = 0
    vMg_prev = [0, 0]
    a_wprev = a_small
    v_wcvt_prev = 0
    proj_pe_prev = 0

    def eslot(u, h):
        return (u % EDEPTH) * 1024 + 512 * h

    for pair in range(NPAIR):
        cvt = {}
        for wi, (src, dst) in enumerate([(wq_in, wq_bf), (wk_in, wk_bf),
                                         (wv_in, wv_bf)]):
            wl = [("sDVE", cvt[wi - 1])] if wi >= 1 else [("sDVE", v_wcvt_prev)]
            if pair == 0 and wi == 0:
                wl = wl + [("sPOOL", g_init)]
            a_w = dma_a(wl, wstage[:, :], src[pair, :, :])
            cvt[wi] = emit("vector",
                           [("sDA", a_w), ("sPE", proj_pe_prev)],
                           lambda v, dst=dst: v.tensor_copy(dst[:, :],
                                                            wstage[:, :]),
                           inc="sDVE")
        v_wcvt = cvt[2]
        v_wcvt_prev = v_wcvt

        ev = {}
        first_wl = [("sDVE", max(v_wcvt, vXn_all)), ("sDVE", vMgb),
                    ("sDVE", vOt_last)]
        gi = 0
        for which, (wbf, dst) in enumerate([(wk_bf, kp), (wq_bf, qp),
                                            (wv_bf, vf)]):
            for nch in range(4 if which == 1 else 8):
                bank = pb[gi % 2]
                for ccx in range(8):
                    wl = []
                    if gi == 0 and ccx == 0:
                        wl = first_wl
                    if gi >= 2 and ccx == 0:
                        wl = [("sDVE", ev[gi - 2])]
                    emit("tensor", wl,
                         lambda t, wbf=wbf, nch=nch, ccx=ccx, bank=bank:
                         t.matmul(bank[:, :],
                                  wbf[:, 128 * ccx:128 * ccx + 128],
                                  xn1[:, T * ccx + 512 * nch:
                                      T * ccx + 512 * nch + 512],
                                  start=(ccx == 0), stop=(ccx == 7)),
                         inc="sPE" if ccx == 7 else None)
                pG = cnt["sPE"]
                ev[gi] = emit("vector", [("sPE", pG)],
                              lambda v, dst=dst, nch=nch, bank=bank: v.tensor_copy(
                                  dst[:, 512 * nch:512 * nch + 512], bank[:, :]),
                              inc="sDVE")
                gi += 1
        proj_evac = cnt["sDVE"]
        vVt = {-2: 0, -1: 0}
        for j in range(NT):
            bank = bfA if j % 2 == 0 else bfB
            wl = [("sDVE", proj_evac), ("sDVE", vXn_all)] if j == 0 else []
            wl = wl + ([("sDVE", vVt[j - 2])] if j >= 2 else [])
            pV = emit("tensor", wl,
                      lambda t, j=j, bank=bank: t.transpose(
                          bank[:, 0:128], vf[:, 128 * j:128 * j + 128],
                          ident_bf[:, :]),
                      inc="sPE")
            vVt[j] = emit("vector", [("sPE", pV)],
                          lambda v, j=j, bank=bank: v.tensor_copy(
                              vp[:, 128 * j:128 * j + 128], bank[:, 0:128]),
                          inc="sDVE")
        vVp = cnt["sDVE"]
        proj_pe_prev = cnt["sPE"]

        vMg = {}
        prevmin = vMg_prev[1]
        for h in range(2):
            hid = 2 * pair + h
            for hf in range(2):
                aMg = emit("scalar",
                           [("sDVE", vMgb), ("sPOOL", g_init),
                            ("sDVE", prevmin)],
                           lambda sc, hid=hid, hf=hf: sc.activation(
                               mscr[:, :], rampT[:, 2048 * hf:2048 * hf + 2048],
                               AF.Relu, bias=mgb[:, hid:hid + 1],
                               scale=-1.0 / R_SPAN),
                           inc="sACT")
                prevmin = emit(
                    "vector", [("sACT", aMg), ("sDVE", vOt_last)],
                    lambda v, h=h, hf=hf: v.tensor_scalar_min(
                        maskT[:, 4096 * h + 2048 * hf:4096 * h + 2048 * hf + 2048],
                        mscr[:, :], 1.0),
                    inc="sDVE")
            vMg[h] = prevmin
        vMg_prev = [vMg[0], vMg[1]]

        if dbg and pair == 0:
            emit("sync", [("sDVE", vVp)],
                 lambda s: s.dma_start(out=dbg_t["dbg_kp"], in_=kp[:, 0:256]),
                 inc="sDS", by=16)
            emit("sync", [],
                 lambda s: s.dma_start(out=dbg_t["dbg_qp"], in_=qp[:, 0:256]),
                 inc="sDS", by=16)
            emit("sync", [],
                 lambda s: s.dma_start(out=dbg_t["dbg_vp"], in_=vp[:, 0:256]),
                 inc="sDS", by=16)
            emit("sync", [("sDVE", vMg[1])],
                 lambda s: s.dma_start(out=dbg_t["dbg_mask"], in_=maskT[:, 0:512]),
                 inc="sDS", by=16)

        for k in range(NCHQ):
            for bi, bank in enumerate([pb[4], pb[5]]):
                emit("tensor",
                     ([("sDVE", vOt_last), ("sPOOL", g_init)]
                      if bi == 0 else []),
                     lambda t, bank=bank: t.matmul(
                         bank[:, :], zeros_bf[:, :], zclr[:, :],
                         start=True, stop=False))
            first_u_of_chunk = u_global
            for j in range(4 * k, NT):
                u = u_global
                m = min(4, j - 4 * k + 1)
                n = 128 * m
                sb0 = pb[(u % 2) * 2]
                sb1 = pb[(u % 2) * 2 + 1]
                wl = []
                if u >= 2:
                    wl.append(("sACT", sE[u - 2]))
                if u == first_u_of_chunk:
                    wl.append(("sDVE", vOt_last))
                    wl.append(("sDVE", vVp))
                    wl.append(("sDVE", vMg[1]))
                emit("tensor", wl,
                     lambda t, j=j, k=k, n=n, sb0=sb0:
                     t.matmul(sb0[:, 0:n], kp[0:64, 128 * j:128 * j + 128],
                              qp[0:64, 512 * k:512 * k + n],
                              start=True, stop=True, tile_position=(0, 0)))
                sS[u] = emit("tensor", [],
                             lambda t, j=j, k=k, n=n, sb1=sb1:
                             t.matmul(sb1[:, 0:n],
                                      kp[64:128, 128 * j:128 * j + 128],
                                      qp[64:128, 512 * k:512 * k + n],
                                      start=True, stop=True,
                                      tile_position=(64, 0)),
                             inc="sPE")
                for h, sbk in enumerate([sb0, sb1]):
                    wl = []
                    if h == 0:
                        wl = [("sPE", sS[u]), ("sPE", sQd[u - EDEPTH])]
                    upper = j >= 16
                    if h == 0 and u == first_u_of_chunk:
                        wl.append(("sDVE", v_ebias))
                    sE[u] = emit("scalar", wl,
                                 lambda sc, u=u, h=h, n=n, sbk=sbk, upper=upper:
                                 sc.activation(
                                     e_ring[:, eslot(u, h):eslot(u, h) + n],
                                     sbk[:, 0:n], AF.Exp,
                                     bias=(hh9[:, 0:1] if upper else -CSHIFT),
                                     scale=0.125),
                                 inc="sACT")
                diag = j < 4 * k + 4
                sCl = {}
                if diag:
                    bidx = m - 1
                    for h in range(2):
                        sCl[h] = emit("gpsimd", [("sACT", sE[u] - (1 - h))],
                                      lambda g, u=u, h=h, bidx=bidx: g.affine_select(
                                          e_ring[:, eslot(u, h) + 128 * bidx:
                                                 eslot(u, h) + 128 * bidx + 128],
                                          e_ring[:, eslot(u, h) + 128 * bidx:
                                                 eslot(u, h) + 128 * bidx + 128],
                                          [[-1, 128]], ALU.is_ge, 0.0, base=0,
                                          channel_multiplier=1),
                                      inc="sPOOL")
                for h in range(2):
                    wl = [("sPOOL", sCl[h])] if diag else \
                        [("sACT", sE[u] - (1 - h))]
                    if h == 0:
                        wl.append(("sPE", sQd[u - EDEPTH]))
                    moff = 4096 * h + 3968 - 128 * (j - 4 * k)
                    sP[(u, h)] = emit(
                        "vector", wl,
                        lambda v, u=u, h=h, n=n, m=m, moff=moff: v.tensor_tensor(
                            p_ring[:, eslot(u, h):eslot(u, h) + n].rearrange(
                                "p (b c) -> p b c", c=128),
                            e_ring[:, eslot(u, h):eslot(u, h) + n].rearrange(
                                "p (b c) -> p b c", c=128),
                            bass.AP(maskT, moff,
                                    [[2 * 4096, 128], [128, m], [1, 128]]),
                            op=ALU.mult),
                        inc="sDVE")
                lastj = j == NT - 1
                for h in range(2):
                    for c2 in range(2):
                        cg = 64 * h + 32 * c2
                        wl = [("sDVE", sP[(u, 1)])] if (h == 0 and c2 == 0) else []
                        emit("tensor", wl,
                             lambda t, j=j, u=u, h=h, cg=cg, n=n, lastj=lastj:
                             t.matmul(pb[4][cg:cg + 32, 0:n],
                                      vp[:, 128 * j + cg:128 * j + cg + 32],
                                      p_ring[:, eslot(u, h):eslot(u, h) + n],
                                      start=False, stop=lastj,
                                      tile_position=(0, cg)))
                stin = [(0, e_ring, 0), (32, p_ring, 0), (64, e_ring, 1),
                        (96, p_ring, 1)]
                for si, (cg, ring, h) in enumerate(stin):
                    r = emit("tensor", [],
                             lambda t, u=u, cg=cg, ring=ring, h=h, n=n,
                             lastj=lastj:
                             t.matmul(pb[5][cg:cg + 1, 0:n], ones_bf[:, 0:1],
                                      ring[:, eslot(u, h):eslot(u, h) + n],
                                      start=False, stop=lastj,
                                      tile_position=(0, cg)),
                             inc="sPE" if si == 3 else None)
                sQd[u] = r if r is not None else cnt["sPE"]
                u_global += 1
            vSt = emit("vector", [("sPE", sQd[u_global - 1])],
                       lambda v: v.tensor_copy(stat_sb[:, :], pb[5][:, :]),
                       inc="sDVE")
            emit("gpsimd", [("sDVE", vSt)],
                 lambda g: g.tensor_copy(m0r[0:1, :], stat_sb[32:33, :]))
            emit("gpsimd", [],
                 lambda g: g.tensor_copy(z1r[0:1, :], stat_sb[64:65, :]))
            gRw = emit("gpsimd", [],
                       lambda g: g.tensor_copy(m1r[0:1, :], stat_sb[96:97, :]),
                       inc="sPOOL")
            e1 = emit("vector", [("sPOOL", gRw)],
                      lambda v: v.scalar_tensor_tensor(
                          dbc[0:1, :], stat_sb[0:1, :], 1e-8, m0r[0:1, :],
                          op0=ALU.mult, op1=ALU.add),
                      inc="sDVE")
            e2 = emit("vector", [("sDVE", e1)],
                      lambda v: v.reciprocal(dr0[0:1, :], dbc[0:1, :]),
                      inc="sDVE")
            e3 = emit("vector", [("sDVE", e2)],
                      lambda v: v.scalar_tensor_tensor(
                          dbc[0:1, :], z1r[0:1, :], 1e-8, m1r[0:1, :],
                          op0=ALU.mult, op1=ALU.add),
                      inc="sDVE")
            vDn = emit("vector", [("sDVE", e3)],
                       lambda v: v.reciprocal(dr1[0:1, :], dbc[0:1, :]),
                       inc="sDVE")
            emit("tensor", [("sDVE", vDn)],
                 lambda t: t.matmul(pb[0][:, :], onesf[0:1, :], dr0[0:1, :],
                                    start=True, stop=True))
            pBc = emit("tensor", [],
                       lambda t: t.matmul(pb[1][:, :], onesf[0:1, :],
                                          dr1[0:1, :], start=True, stop=True),
                       inc="sPE")
            c1 = emit("vector", [("sPE", pBc)],
                      lambda v: v.tensor_copy(dbc[0:64, :], pb[0][0:64, :]),
                      inc="sDVE")
            c2 = emit("vector", [("sDVE", c1)],
                      lambda v: v.tensor_copy(dbc[64:128, :], pb[1][64:128, :]),
                      inc="sDVE")
            vOt_last = emit("vector", [("sDVE", c2)],
                            lambda v, pair=pair, k=k: v.tensor_tensor(
                                ot[:, OWN_T * pair + 512 * k:
                                   OWN_T * pair + 512 * k + 512],
                                pb[4][:, :], dbc[:, :], op=ALU.mult),
                            inc="sDVE")

    dbg_ring_dma = 0
    if dbg:
        emit("sync", [("sDVE", vOt_last)],
             lambda s: s.dma_start(out=dbg_t["dbg_er"], in_=e_ring[:, :]),
             inc="sDS", by=16)
        dbg_ring_dma = emit("sync", [],
                            lambda s: s.dma_start(out=dbg_t["dbg_pr"],
                                                  in_=p_ring[:, :]),
                            inc="sDS", by=16)

    stA.close()
    wstage2 = mk(stP, "wstage2", [128, 2048], F32)
    wo_bf = mk(stP, "wo_bf", [128, 8192], BF16)
    p4buf = mk(stP, "p4buf", [128, 4096], F32)
    o4sb4 = mk(stP, "o4sb4", [128, 512], F32)

    gWo = 0
    for c0 in range(0, 8192, 2048):
        wl = ([("sDVE", vOt_last), ("sDS", dbg_ring_dma)] if c0 == 0
              else [("sPOOL", gWo)])
        a_wo = dma_a(wl, wstage2[:, :], wo_in[:, c0:c0 + 2048])
        gWo = emit("gpsimd", [("sDA", a_wo)],
                   lambda g, c0=c0: g.tensor_copy(wo_bf[:, c0:c0 + 2048],
                                                  wstage2[:, :]),
                   inc="sPOOL")
    if dbg:
        emit("sync", [("sPOOL", gWo)],
             lambda s: s.dma_start(out=dbg_t["dbg_wo"], in_=wo_bf[:, 0:1024]),
             inc="sDS", by=16)
        emit("sync", [],
             lambda s: s.dma_start(out=dbg_t["dbg_ws2"], in_=wstage2[:, 0:1024]),
             inc="sDS", by=16)

    u1_bf = mk(stP, "u1_bf", [128, 2048], BF16)
    v1_bf = mk(stP, "v1_bf", [128, 8192], BF16)
    u2_bf = mk(stP, "u2_bf", [128, 8192], BF16)
    v2_bf = mk(stP, "v2_bf", [128, 2048], BF16)
    rsb = mk(stP, "rsb", [128, 4096], F32)
    rsbX = mk(stP, "rsbX", [128, 4096], F32)
    xn2 = mk(stP, "xn2", [128, 8 * 512], BF16)
    p1sb = mk(stP, "p1sb", [128, 1024], BF16)
    hsb = mk(stP, "hsb", [128, 32 * 512], BF16)
    q2sb = mk(stP, "q2sb", [128, 1024], BF16)
    f4sb = mk(stP, "f4sb", [128, 512], F32)
    fout = rsbX
    gCv = 0
    first_ffn_load = True
    for (srcw, dst, wdth) in [(u1_in, u1_bf, 2048), (v1_in, v1_bf, 8192),
                              (u2_in, u2_bf, 8192), (v2_in, v2_bf, 2048)]:
        for c0 in range(0, wdth, 2048):
            cw = min(2048, wdth - c0)
            wl = [("sPOOL", gWo)] if first_ffn_load else [("sPOOL", gCv)]
            first_ffn_load = False
            a_w = dma_a(wl, wstage2[:, 0:cw], srcw[:, c0:c0 + cw])
            gCv = emit("gpsimd", [("sDA", a_w)],
                       lambda g, dst=dst, c0=c0, cw=cw: g.tensor_copy(
                           dst[:, c0:c0 + cw], wstage2[:, 0:cw]),
                       inc="sPOOL")

    vTchFree = {-2: 0, -1: 0}
    pT2 = {}
    vX2 = {}
    vV2prev = [0]
    for i in range(-2, 16):
        pT2[i] = 0
        vX2[i] = 0
    dY = 0
    vPt4 = {-2: 0, -1: 0}
    pTr4 = {-1: 0}
    vEv4 = {-2: 0, -1: 0}
    fcg = 0
    vAddsPrev = 0
    for tch in range(4):
        par = tch % 2
        for fc in range(8):
            bank = pb[fc % 2]
            for dc in range(NPAIR):
                wl2 = []
                if fcg == 0 and dc == 0:
                    wl2 = [("sPOOL", gCv), ("sDVE", vOt_last),
                           ("sDA", a_small)]
                if fcg >= 2 and dc == 0:
                    wl2 = [("sDVE", vEv4[fcg - 2])]
                if fc == 0 and dc == 0 and tch >= 1:
                    wl2.append(("sDVE", vTchFree[tch - 1]))
                emit("tensor", wl2,
                     lambda t, tch=tch, fc=fc, dc=dc, bank=bank: t.matmul(
                         bank[:, :],
                         wo_bf[:, 1024 * dc + 128 * fc:1024 * dc + 128 * fc + 128],
                         ot[:, OWN_T * dc + 512 * tch:OWN_T * dc + 512 * tch + 512],
                         start=(dc == 0), stop=(dc == NPAIR - 1)),
                     inc="sPE" if dc == NPAIR - 1 else None)
            pG4 = cnt["sPE"]
            vEv4[fcg] = emit("vector", [("sPE", pG4), ("sPE", pTr4[fcg - 1])],
                             lambda v, fc=fc, bank=bank: v.tensor_scalar(
                                 o4sb4[:, :], bank[:, :], bo_sb[:, fc:fc + 1],
                                 None, op0=ALU.add),
                             inc="sDVE")
            tb = pb[2 + fc % 2]
            for s4 in range(4):
                wl3 = []
                if s4 == 0:
                    wl3 = [("sDVE", vEv4[fcg]), ("sDVE", vPt4[fcg - 2])]
                emit("tensor", wl3,
                     lambda t, s4=s4, tb=tb: t.transpose(
                         tb[:, 128 * s4:128 * s4 + 128],
                         o4sb4[:, 128 * s4:128 * s4 + 128], identf[:, :]),
                     inc="sPE" if s4 == 3 else None)
            pTr4[fcg] = cnt["sPE"]
            wl3 = [("sPE", pTr4[fcg])]
            if fc == 0:
                wl3.append(("sDVE", vAddsPrev))
            vPt4[fcg] = emit(
                "vector", wl3,
                lambda v, fc=fc, tb=tb: v.tensor_copy(
                    bass.AP(p4buf, 128 * fc, [[4096, 128], [1024, 4], [1, 128]]),
                    bass.AP(tb, 0, [[512, 128], [128, 4], [1, 128]])),
                inc="sDVE")
            if dbg and fcg == 0:
                emit("sync", [("sDVE", vEv4[0])],
                     lambda s: s.dma_start(out=dbg_t["dbg_o4"], in_=o4sb4[:, :]),
                     inc="sDS", by=16)
                emit("sync", [],
                     lambda s: s.dma_start(out=dbg_t["dbg_boh"], in_=bo_sb[:, :]),
                     inc="sDS", by=16)
            fcg += 1
        vP4 = cnt["sDVE"]
        if dbg and tch == 0:
            emit("sync", [("sDVE", vP4)],
                 lambda s: s.dma_start(out=dbg_t["dbg_p4"], in_=p4buf[:, 0:1024]),
                 inc="sDS", by=16)

        dRes = 0
        vAdds = 0
        for s4 in range(4):
            wl = []
            if s4 == 0:
                wl = [("sDVE", vTchFree[tch - 1]), ("sPOOL", gCv)]
            dRes = emit("sync", wl,
                        lambda s, tch=tch, s4=s4: s.dma_start(
                            out=rsbX[:, 1024 * s4:1024 * s4 + 1024],
                            in_=xres_in[512 * tch + 128 * s4:
                                        512 * tch + 128 * s4 + 128, :]),
                        inc="sDS", by=16)
            vAdds = emit("vector",
                         [("sDS", dRes)] + ([("sDVE", vP4)] if s4 == 0 else []),
                         lambda v, s4=s4: v.tensor_tensor(
                             rsb[:, 1024 * s4:1024 * s4 + 1024],
                             rsbX[:, 1024 * s4:1024 * s4 + 1024],
                             p4buf[:, 1024 * s4:1024 * s4 + 1024],
                             op=ALU.add),
                         inc="sDVE")
        vAddsPrev = vAdds
        vRes = cnt["sDVE"]


        for s4l in range(4):
            s4 = 4 * tch + s4l
            base = 1024 * s4l
            sp = s4 % 2

            aSq2 = emit("scalar", [("sDVE", vRes), ("sDVE", vX2[s4 - 2]),
                                    ("sDVE", vV2prev[0])],
                        lambda sc, base=base: sc.activation(
                            sqscr[:, :], rsb[:, base:base + D], AF.Square),
                        inc="sACT")
            r1 = emit("vector", [("sDVE", vRes)] if s4 == 0 else [],
                      lambda v, sp=sp, base=base: v.reduce_sum(
                          srow[:, 2 * sp:2 * sp + 1], rsb[:, base:base + D],
                          axis=AX.X),
                      inc="sDVE")
            r2 = emit("vector", [("sACT", aSq2)],
                      lambda v, sp=sp: v.reduce_sum(
                          srow[:, 2 * sp + 1:2 * sp + 2], sqscr[:, :], axis=AX.X),
                      inc="sDVE")
            r3 = emit("vector", [("sDVE", r1)],
                      lambda v, sp=sp: v.tensor_scalar_mul(
                          mrow[:, sp:sp + 1], srow[:, 2 * sp:2 * sp + 1], 1.0 / D),
                      inc="sDVE")
            r4 = emit("vector", [("sDVE", r3)],
                      lambda v, sp=sp: v.tensor_tensor(
                          msq[:, sp:sp + 1], mrow[:, sp:sp + 1], mrow[:, sp:sp + 1],
                          op=ALU.mult),
                      inc="sDVE")
            vV2 = emit("vector", [("sDVE", r4), ("sDVE", r2)],
                       lambda v, sp=sp: v.scalar_tensor_tensor(
                           varr[:, sp:sp + 1], srow[:, 2 * sp + 1:2 * sp + 2],
                           1.0 / D, msq[:, sp:sp + 1],
                           op0=ALU.mult, op1=ALU.subtract),
                       inc="sDVE")
            vV2prev[0] = vV2
            aLn2 = emit("scalar", [("sDVE", vV2)],
                        lambda sc, sp=sp: sc.activation(lnr[:, sp:sp + 1],
                                                        varr[:, sp:sp + 1], AF.Ln,
                                                        bias=EPS, scale=1.0),
                        inc="sACT")
            aR2 = emit("scalar", [("sACT", aLn2)],
                       lambda sc, sp=sp: sc.activation(
                           rstd[:, sp:sp + 1], lnr[:, sp:sp + 1], AF.Exp,
                           bias=0.0, scale=-0.5),
                       inc="sACT")
            vXt2 = emit("vector", [("sACT", aR2), ("sPE", pT2[s4 - 2])],
                        lambda v, sp=sp, base=base: v.tensor_scalar(
                            xtok[:, sp * D:sp * D + D], rsb[:, base:base + D],
                            mrow[:, sp:sp + 1], rstd[:, sp:sp + 1],
                            op0=ALU.subtract, op1=ALU.mult),
                        inc="sDVE")
            bank = bfA if sp == 0 else bfB
            for ccx in range(8):
                wl4 = [("sDVE", vXt2), ("sDVE", vX2[s4 - 2])] if ccx == 0 else []
                emit("tensor", wl4,
                     lambda t, sp=sp, ccx=ccx, bank=bank: t.transpose(
                         bank[:, 128 * ccx:128 * ccx + 128],
                         xtok[:, sp * D + 128 * ccx:sp * D + 128 * ccx + 128],
                         ident_bf[:, :]),
                     inc="sPE" if ccx == 7 else None)
            pT2[s4] = cnt["sPE"]
            for half in range(2):
                emit("vector", [("sPE", pT2[s4])] if half == 0 else [],
                     lambda v, s4l=s4l, half=half, bank=bank: v.tensor_copy(
                         bass.AP(xn2, 512 * 4 * half + 128 * s4l,
                                 [[8 * 512, 128], [512, 4], [1, 128]]),
                         bass.AP(bank, 512 * half,
                                 [[1024, 128], [128, 4], [1, 128]])),
                     inc="sDVE" if half == 1 else None)
            vX2[s4] = cnt["sDVE"]
        vXn2 = cnt["sDVE"]

        vP1e = {}
        for rc in range(2):
            bank = pb[rc]
            for dc in range(8):
                emit("tensor",
                     [("sDVE", vXn2)] if (rc == 0 and dc == 0) else [],
                     lambda t, rc=rc, dc=dc, bank=bank: t.matmul(
                         bank[:, :],
                         u1_bf[:, 256 * dc + 128 * rc:256 * dc + 128 * rc + 128],
                         xn2[:, 512 * dc:512 * dc + 512],
                         start=(dc == 0), stop=(dc == 7)),
                     inc="sPE" if dc == 7 else None)
            pP1 = cnt["sPE"]
            vP1e[rc] = emit("vector", [("sPE", pP1)],
                            lambda v, rc=rc, bank=bank: v.tensor_copy(
                                p1sb[:, 512 * rc:512 * rc + 512], bank[:, :]),
                            inc="sDVE")
        vP1 = cnt["sDVE"]
        aH = {-2: 0, -1: 0}
        for fch in range(32):
            bank = pb[fch % 2]
            for rc in range(2):
                wl5 = []
                if fch == 0 and rc == 0:
                    wl5 = [("sDVE", vP1)]
                if fch >= 2 and rc == 0:
                    wl5 = [("sACT", aH[fch - 2])]
                emit("tensor", wl5,
                     lambda t, fch=fch, rc=rc, bank=bank: t.matmul(
                         bank[:, :],
                         v1_bf[:, 4096 * rc + 128 * fch:
                               4096 * rc + 128 * fch + 128],
                         p1sb[:, 512 * rc:512 * rc + 512],
                         start=(rc == 0), stop=(rc == 1)),
                     inc="sPE" if rc == 1 else None)
            pH = cnt["sPE"]
            aH[fch] = emit("scalar", [("sPE", pH)],
                           lambda sc, fch=fch, bank=bank: sc.activation(
                               hsb[:, 512 * fch:512 * fch + 512], bank[:, :],
                               AF.Gelu_apprx_tanh, bias=b1_sb[:, fch:fch + 1],
                               scale=1.0),
                           inc="sACT")
        aH_all = aH[31]
        for rc in range(2):
            bank = pb[2 + rc]
            for fch in range(32):
                wl6 = [("sACT", aH_all)] if (rc == 0 and fch == 0) else []
                emit("tensor", wl6,
                     lambda t, rc=rc, fch=fch, bank=bank: t.matmul(
                         bank[:, :],
                         u2_bf[:, 256 * fch + 128 * rc:256 * fch + 128 * rc + 128],
                         hsb[:, 512 * fch:512 * fch + 512],
                         start=(fch == 0), stop=(fch == 31)),
                     inc="sPE" if fch == 31 else None)
            pQ2 = cnt["sPE"]
            emit("vector", [("sPE", pQ2)],
                 lambda v, rc=rc, bank=bank: v.tensor_copy(
                     q2sb[:, 512 * rc:512 * rc + 512], bank[:, :]),
                 inc="sDVE")
        vQ2 = cnt["sDVE"]
        pTf = {-1: 0}
        vFe = {-2: 0, -1: 0}
        vFadd = {-2: 0, -1: 0}
        for fc in range(8):
            bank = pb[fc % 2]
            for rc in range(2):
                wl7 = []
                if fc == 0 and rc == 0:
                    wl7 = [("sDVE", vQ2)]
                if fc >= 2 and rc == 0:
                    wl7 = [("sDVE", vFe[fc - 2])]
                emit("tensor", wl7,
                     lambda t, fc=fc, rc=rc, bank=bank: t.matmul(
                         bank[:, :],
                         v2_bf[:, 1024 * rc + 128 * fc:1024 * rc + 128 * fc + 128],
                         q2sb[:, 512 * rc:512 * rc + 512],
                         start=(rc == 0), stop=(rc == 1)),
                     inc="sPE" if rc == 1 else None)
            pF4 = cnt["sPE"]
            vFe[fc] = emit("vector", [("sPE", pF4), ("sPE", pTf[fc - 1])],
                           lambda v, fc=fc, bank=bank: v.tensor_scalar(
                               f4sb[:, :], bank[:, :], b2_sb[:, fc:fc + 1], None,
                               op0=ALU.add),
                           inc="sDVE")
            tb = pb[2 + fc % 2]
            for s4 in range(4):
                wl8 = []
                if s4 == 0:
                    wl8 = [("sDVE", vFe[fc]), ("sDVE", vFadd[fc - 2])]
                emit("tensor", wl8,
                     lambda t, s4=s4, tb=tb: t.transpose(
                         tb[:, 128 * s4:128 * s4 + 128],
                         f4sb[:, 128 * s4:128 * s4 + 128], identf[:, :]),
                     inc="sPE" if s4 == 3 else None)
            pTf[fc] = cnt["sPE"]
            wl8 = [("sPE", pTf[fc])]
            if fc == 0 and tch >= 1:
                wl8.append(("sDS", dY))
            vFadd[fc] = emit(
                "vector", wl8,
                lambda v, fc=fc, par=par, tb=tb: v.tensor_tensor(
                    bass.AP(fout, 128 * fc,
                            [[4096, 128], [1024, 4], [1, 128]]),
                    bass.AP(rsb, 128 * fc,
                            [[4096, 128], [1024, 4], [1, 128]]),
                    bass.AP(tb, 0, [[512, 128], [128, 4], [1, 128]]),
                    op=ALU.add),
                inc="sDVE")
        vTchFree[tch] = cnt["sDVE"]
        for s4 in range(4):
            dY = emit("sync", [("sDVE", vTchFree[tch])] if s4 == 0 else [],
                      lambda s, tch=tch, s4=s4, par=par: s.dma_start(
                          out=y_out[512 * tch + 128 * s4:
                                    512 * tch + 128 * s4 + 128, :],
                          in_=fout[:, 1024 * s4:1024 * s4 + 1024]),
                      inc="sDS", by=16)

    if dbg:
        dstage = mk(stP, "dstage", [128, 256], F32)
        dump_prev = [0]

        def dump(name, src_ap, wait):
            w = src_ap.shape[-1]
            for c0 in range(0, w, 256):
                cw = min(256, w - c0)
                wl = ([wait] if wait else []) + [("sDS", dump_prev[0])]
                vc = emit("vector", wl,
                          lambda v, src_ap=src_ap, c0=c0, cw=cw: v.tensor_copy(
                              dstage[:, 0:cw], src_ap[:, c0:c0 + cw]),
                          inc="sDVE")
                dump_prev[0] = emit("sync", [("sDVE", vc)],
                                    lambda s, name=name, c0=c0, cw=cw:
                                    s.dma_start(out=dbg_t[name][:, c0:c0 + cw],
                                                in_=dstage[:, 0:cw]),
                                    inc="sDS", by=16)
        emit("sync", [("sDVE", vOt_last)],
             lambda s: s.dma_start(out=dbg_t["dbg_ot"], in_=ot[:, :]),
             inc="sDS", by=16)
        dump("dbg_rsb", rsb[:, 0:1024], None)
        dump("dbg_rsb2", rsb[:, 2048:4096], None)
        emit("sync", [("sDS", dump_prev[0])],
             lambda s: s.dma_start(out=dbg_t["dbg_xn2b"], in_=xn2[:, 512:1024]),
             inc="sDS", by=16)
        emit("vector", [("sDS", dump_prev[0])],
             lambda v: v.tensor_copy(dstage[:, 0:2], mrow[:, 0:2]))
        vln2 = emit("vector", [],
                    lambda v: v.tensor_copy(dstage[:, 2:4], rstd[:, 0:2]),
                    inc="sDVE")
        dump_prev[0] = emit("sync", [("sDVE", vln2)],
                            lambda s: s.dma_start(out=dbg_t["dbg_ln2"],
                                                  in_=dstage[:, 0:8]),
                            inc="sDS", by=16)
        emit("sync", [],
             lambda s: s.dma_start(out=dbg_t["dbg_xn2"], in_=xn2[:, 0:512]),
             inc="sDS", by=16)
        emit("sync", [],
             lambda s: s.dma_start(out=dbg_t["dbg_fout"], in_=fout[:, :]),
             inc="sDS", by=16)
        emit("sync", [],
             lambda s: s.dma_start(out=dbg_t["dbg_h"], in_=hsb[:, 0:512]),
             inc="sDS", by=16)
    dY_final = cnt["sDS"]
    ops["sync"].append(lambda s: s.wait_ge(sems["sDS"], dY_final))

    @block.sync
    def _(e):
        for f in ops["sync"]:
            f(e)

    @block.scalar
    def _(e):
        for f in ops["scalar"]:
            f(e)

    @block.vector
    def _(e):
        for f in ops["vector"]:
            f(e)

    @block.tensor
    def _(e):
        for f in ops["tensor"]:
            f(e)

    @block.gpsimd
    def _(e):
        for f in ops["gpsimd"]:
            f(e)

    block_cm.__exit__(None, None, None)
    pbs.close()
    stP.close()
    sst.close()
    return nc


def _prep_shared(Wq, Wk, Wv, Wo, bo, Wspan, bspan, U1, V1, b1, U2, V2, b2):
    f32 = np.float32
    c = np.ascontiguousarray
    sh = {}
    for name, W in [("wq", Wq), ("wk", Wk), ("wv", Wv)]:
        WT = W.T
        pairs = []
        for gp in range(NPAIR):
            sl = WT[:, 128 * gp:128 * gp + 128]
            pairs.append(sl.reshape(8, 128, 128).transpose(1, 0, 2)
                         .reshape(128, 1024))
        sh[name] = c(np.stack(pairs)).astype(f32)
    sh["wo"] = c(Wo.T.reshape(8, 128, 8, 128).transpose(1, 0, 2, 3)
                 .reshape(128, 8192)).astype(f32)
    sh["wspan"] = c(Wspan.T.reshape(8, 128, 16).transpose(1, 0, 2)
                    .reshape(128, 128)).astype(f32)
    sh["bspan"] = c(bspan.reshape(1, 16)).astype(f32)
    sh["bo"] = c(bo.reshape(8, 128).T).astype(f32)
    sh["u1"] = c(U1.reshape(8, 128, 2, 128).transpose(1, 0, 2, 3)
                 .reshape(128, 2048)).astype(f32)
    sh["v1"] = c(V1.reshape(2, 128, 32, 128).transpose(1, 0, 2, 3)
                 .reshape(128, 8192)).astype(f32)
    sh["b1"] = c(b1.reshape(32, 128).T).astype(f32)
    sh["u2"] = c(U2.reshape(32, 128, 2, 128).transpose(1, 0, 2, 3)
                 .reshape(128, 8192)).astype(f32)
    sh["v2"] = c(V2.reshape(2, 128, 8, 128).transpose(1, 0, 2, 3)
                 .reshape(128, 2048)).astype(f32)
    sh["b2"] = c(b2.reshape(8, 128).T).astype(f32)
    sh["ident"] = np.eye(128, dtype=f32)
    return sh


def kernel(L0_attn_x, L0_attn_Wq, L0_attn_Wk, L0_attn_Wv, L0_attn_Wo, L0_attn_bo,
           L0_attn_Wspan, L0_attn_bspan, L0_ffn_U1, L0_ffn_V1, L0_ffn_b1,
           L0_ffn_U2, L0_ffn_V2, L0_ffn_b2):
    x = np.ascontiguousarray(np.asarray(L0_attn_x, dtype=np.float32))
    (Wq, Wk, Wv, Wo, bo, Wspan, bspan, U1, V1, b1, U2, V2, b2) = [
        np.ascontiguousarray(np.asarray(a, dtype=np.float32)) for a in
        (L0_attn_Wq, L0_attn_Wk, L0_attn_Wv, L0_attn_Wo, L0_attn_bo,
         L0_attn_Wspan, L0_attn_bspan, L0_ffn_U1, L0_ffn_V1, L0_ffn_b1,
         L0_ffn_U2, L0_ffn_V2, L0_ffn_b2)]

    shared = _prep_shared(Wq, Wk, Wv, Wo, bo, Wspan, bspan,
                          U1, V1, b1, U2, V2, b2)

    in_maps = []
    for c_id in range(8):
        b, hh = c_id // 2, c_id % 2
        m = dict(shared)
        own = x[b, OWN_T * hh:OWN_T * hh + OWN_T]
        oth = x[b, OWN_T * (1 - hh):OWN_T * (1 - hh) + OWN_T]
        m["x"] = np.ascontiguousarray(np.concatenate([own, oth], axis=0))
        m["x_res"] = np.ascontiguousarray(own)
        m["hhc"] = np.full((128, 1), float(hh), dtype=np.float32)
        in_maps.append(m)

    nc = build_program(dbg=_DBG)
    res = run_bass_kernel_spmd(nc, in_maps, core_ids=list(range(8)))
    out = np.empty((B, T, D), dtype=np.float32)
    for c_id in range(8):
        b, hh = c_id // 2, c_id % 2
        out[b, OWN_T * hh:OWN_T * hh + OWN_T] = res.results[c_id]["y"]
    if _DBG:
        return out, res.results
    return out
